# version 02
# baseline (speedup 1.0000x reference)
import numpy as np
import ml_dtypes
import concourse.bass as bass
import concourse.mybir as mybir



AF = mybir.ActivationFunctionType
ALU = mybir.AluOpType
AX = mybir.AxisListType
F32 = mybir.dt.float32
BF16 = mybir.dt.bfloat16
BF = ml_dtypes.bfloat16

D = 1024
DCH = 8
T = 8192
NT = 16
H = 16
HD = 64
FF = 4096
FCH = 32
SEQ = 4096
CH = 512
NC_CH = 8
LN_EPS = 1e-5
ATTN_EPS = 1e-6


def ctx_head_off(b, h):
    return 64 * (h % 2), b * 520 + (h // 8) * 260 + ((h // 2) % 4) * 65


class S:

    def __init__(self, nc, es, name):
        self.h = es.enter_context(nc.semaphore(name))
        self.n = 0

    def inc(self, ins, k=1):
        ins.then_inc(self.h, k)
        self.n += k
        return ins


def mk_sems(nc, es, names, pfx=""):
    return {n: S(nc, es, pfx + "_" + n) for n in names}


def emit_p0(nc, es, xT, WkvT, bkv, ctx_out, pfx="p0", n_chunks=64, dbg=None):
    sd = es.enter_context(nc.semaphore(pfx + "_sd"))
    sx = es.enter_context(nc.semaphore(pfx + "_sx"))
    sg = es.enter_context(nc.semaphore(pfx + "_sg"))
    mm_kv = es.enter_context(nc.semaphore(pfx + "_mmkv"))
    act_k = es.enter_context(nc.semaphore(pfx + "_actk"))
    dve_k = es.enter_context(nc.semaphore(pfx + "_dvek"))
    ctx_done = es.enter_context(nc.semaphore(pfx + "_ctxd"))
    act_ctx = es.enter_context(nc.semaphore(pfx + "_actctx"))
    st = es.enter_context(nc.semaphore(pfx + "_st"))

    with (
        nc.sbuf_tensor(pfx + "_wkv", [128, 8 * 2048], BF16) as wkv,
        nc.sbuf_tensor(pfx + "_bkv", [1, 2048], BF16) as bkv_sb,
        nc.sbuf_tensor(pfx + "_ones1", [1, 128], BF16) as ones1,
        nc.sbuf_tensor(pfx + "_xbf", [128, 8 * 512], BF16) as xbf,
        nc.sbuf_tensor(pfx + "_e", [128, 1024], F32) as e_sb,
        nc.sbuf_tensor(pfx + "_z1", [128, 1024], F32) as z1_sb,
        nc.sbuf_tensor(pfx + "_k", [128, 1024], BF16) as k_sb,
        nc.sbuf_tensor(pfx + "_v65", [128, 1040], BF16) as v65,
        nc.sbuf_tensor(pfx + "_ctxsb", [128, 1040], BF16) as ctx_sb,
        nc.psum_tensor(pfx + "_kvps", [128, 2048], F32) as kv_ps,
        nc.psum_tensor(pfx + "_ctx00", [128, 260], F32) as ctx00,
        nc.psum_tensor(pfx + "_ctx01", [128, 260], F32) as ctx01,
        nc.psum_tensor(pfx + "_ctx10", [128, 260], F32) as ctx10,
        nc.psum_tensor(pfx + "_ctx11", [128, 260], F32) as ctx11,
        nc.Block() as blk,
    ):
        ctx_ps = {(0, 0): ctx00, (0, 1): ctx01, (1, 0): ctx10, (1, 1): ctx11}

        @blk.gpsimd
        def _(g):
            for d in range(DCH):
                g.dma_start(out=wkv[:, d * 2048:(d + 1) * 2048],
                            in_=WkvT[d * 128:(d + 1) * 128, :]).then_inc(sd, 16)
            g.dma_start(out=bkv_sb[:, :], in_=bkv[:, :]).then_inc(sd, 16)
            g.memset(ones1[:, :], 1.0)
            g.memset(v65[:, :], 1.0).then_inc(sg, 1)
            for t in range((n_chunks + 3) // 4):
                if t > 0:
                    g.wait_ge(mm_kv, 4 * t)
                for d in range(DCH):
                    g.dma_start(out=xbf[:, d * 512:(d + 1) * 512],
                                in_=xT[d * 128:(d + 1) * 128, t * 512:(t + 1) * 512]
                                ).then_inc(sx, 16)
            g.wait_ge(act_ctx, 1)
            g.dma_start(out=ctx_out[:, :], in_=ctx_sb[:, :]).then_inc(st, 16)
            nst = 16
            if dbg is not None:
                for name, sb in [("k", k_sb), ("v65", v65), ("e", e_sb),
                                 ("z1", z1_sb), ("xbf", xbf)]:
                    if name in dbg:
                        g.dma_start(out=dbg[name][:, :], in_=sb[:, :]).then_inc(st, 16)
                        nst += 16
            g.wait_ge(st, nst)

        @blk.tensor
        def _(t_):
            t_.wait_ge(sd, 16 * 9)
            t_.wait_ge(sg, 1)
            for i in range(n_chunks):
                t = i // 4
                kk = i % 4
                b = i // 32
                t_.wait_ge(sx, 16 * 8 * (t + 1))
                lhs_off = kk * 128
                for seg in range(4):
                    t_.matmul(kv_ps[:, seg * 512:(seg + 1) * 512],
                              ones1[:, :],
                              bkv_sb[:, seg * 512:(seg + 1) * 512],
                              start=True, stop=False)
                for d in range(DCH):
                    lhsT = xbf[:, d * 512 + lhs_off: d * 512 + lhs_off + 128]
                    for seg in range(4):
                        ins = t_.matmul(kv_ps[:, seg * 512:(seg + 1) * 512],
                                        lhsT,
                                        wkv[:, d * 2048 + seg * 512: d * 2048 + (seg + 1) * 512],
                                        start=False, stop=(d == DCH - 1))
                ins.then_inc(mm_kv, 1)
                t_.wait_ge(act_k, i + 1)
                t_.wait_ge(dve_k, i + 1)
                for h in range(H):
                    pb, fo = ctx_head_off(0, h)
                    ps = ctx_ps[(b, h // 8)]
                    slot = (h // 2) % 4
                    ins = t_.matmul(ps[pb:pb + 64, slot * 65:(slot + 1) * 65],
                                    k_sb[:, h * 64:(h + 1) * 64],
                                    v65[:, h * 65:(h + 1) * 65],
                                    start=(i % 32 == 0 and (h % 8) < 2),
                                    stop=(i % 32 == 31 or i == n_chunks - 1))
                ins.then_inc(ctx_done, 1)

        @blk.scalar
        def _(s):
            for i in range(n_chunks):
                s.wait_ge(mm_kv, i + 1)
                if i > 0:
                    s.wait_ge(dve_k, i)
                s.activation(e_sb[:, :], kv_ps[:, 0:1024], AF.Exp)
                s.activation(z1_sb[:, :], kv_ps[:, 0:1024], AF.Copy,
                             bias=1.0).then_inc(act_k, 1)
            s.wait_ge(ctx_done, n_chunks)
            for (b, j), ps in ctx_ps.items():
                ins = s.copy(ctx_sb[:, b * 520 + j * 260: b * 520 + (j + 1) * 260],
                             ps[:, :])
            ins.then_inc(act_ctx, 1)

        @blk.vector
        def _(v):
            v.wait_ge(sg, 1)
            for i in range(n_chunks):
                v.wait_ge(act_k, i + 1)
                if i > 0:
                    v.wait_ge(ctx_done, i)
                v.scalar_tensor_tensor(k_sb[:, :], e_sb[:, :], 1.0, z1_sb[:, :],
                                       ALU.min, ALU.max)
                v.tensor_copy(
                    v65[:, :].rearrange("p (h s) -> p h s", h=16)[:, :, 0:64],
                    kv_ps[:, 1024:2048].rearrange("p (h s) -> p h s", h=16),
                ).then_inc(dve_k, 1)


def prep_core_x(x_full, c):
    xc = x_full[2 * c:2 * c + 2].reshape(2 * SEQ, D)
    return np.ascontiguousarray(xc.T)


def prep_wkv(Wk, Wv):
    return np.ascontiguousarray(np.concatenate([Wk.T, Wv.T], axis=1))


def prep_bkv(bk, bv):
    return np.ascontiguousarray(np.concatenate([bk, bv])[None, :])



class GP:

    def __init__(self, eng):
        self.__dict__["e"] = eng
        self.__dict__["on"] = True

    def __setattr__(self, k, v):
        self.__dict__[k] = v

    def __getattr__(self, name):
        fn = getattr(self.__dict__["e"], name)
        if name in ("wait_ge", "nop") or not callable(fn):
            return fn

        def wrap(*a, **k):
            if self.__dict__["on"]:
                return fn(*a, **k)
            return self.__dict__["e"].nop()
        return wrap



P1_SEMS = ["sw", "sxf", "sxb", "sctx", "mmq", "actq", "dveq", "mmden",
           "dveden", "mmr", "actr", "mmnum", "dveop", "mmo", "dvey",
           "dvecast", "actys", "mmstat", "actm2", "dvevar", "actrs",
           "mmbc", "dveapply", "str1", "dvekb", "dveinit"]

CINV = 1.0 / 1024.0


def emit_p1(nc, es, sm, xT, ctx_in, WqT_d, bq_d, WoT_d, bo_d, E_d, r1_out,
            pfx="p1", n_tiles=16, en=99):
    b = {k: s.n for k, s in sm.items()}

    from contextlib import ExitStack
    with ExitStack() as es2:
        wq_sb = es2.enter_context(nc.sbuf_tensor(pfx + "_wq", [128, 8 * 1024], BF16))
        wo_sb = es2.enter_context(nc.sbuf_tensor(pfx + "_wo", [128, 8 * 1024], BF16))
        bq_sb = es2.enter_context(nc.sbuf_tensor(pfx + "_bq", [1, 1024], BF16))
        bo_sb = es2.enter_context(nc.sbuf_tensor(pfx + "_bo", [1, 1024], BF16))
        ctx_sb = es2.enter_context(nc.sbuf_tensor(pfx + "_ctx", [128, 1040], BF16))
        E_sb = es2.enter_context(nc.sbuf_tensor(pfx + "_E", [16, 1024], BF16))
        KB_sb = es2.enter_context(nc.sbuf_tensor(pfx + "_KB", [128, 128], BF16))
        ones1f = es2.enter_context(nc.sbuf_tensor(pfx + "_ones1f", [1, 128], F32))
        ones128 = es2.enter_context(nc.sbuf_tensor(pfx + "_ones128", [128, 1], BF16))
        ones512 = es2.enter_context(nc.sbuf_tensor(pfx + "_ones512", [1, 512], BF16))
        xbf = es2.enter_context(nc.sbuf_tensor(pfx + "_xbf", [128, 8 * 512], BF16))
        x_sb = es2.enter_context(nc.sbuf_tensor(pfx + "_x", [128, 8 * 512], F32))
        q_sb = es2.enter_context(nc.sbuf_tensor(pfx + "_q", [128, 8 * 512], BF16))
        e_sb = es2.enter_context(nc.sbuf_tensor(pfx + "_e", [128, 512], F32))
        z1_sb = es2.enter_context(nc.sbuf_tensor(pfx + "_z1", [128, 512], F32))
        den_sb = es2.enter_context(nc.sbuf_tensor(pfx + "_den", [16, 512], F32))
        rec_sb = es2.enter_context(nc.sbuf_tensor(pfx + "_rec", [16, 512], F32))
        rbf = es2.enter_context(nc.sbuf_tensor(pfx + "_rbf", [16, 512], BF16))
        R_sb = es2.enter_context(nc.sbuf_tensor(pfx + "_R", [128, 8 * 512], BF16))
        op_sb = es2.enter_context(nc.sbuf_tensor(pfx + "_op", [128, 8 * 512], BF16))
        y_sb = es2.enter_context(nc.sbuf_tensor(pfx + "_y", [128, 8 * 512], F32))
        ybf = es2.enter_context(nc.sbuf_tensor(pfx + "_ybf", [128, 8 * 512], BF16))
        ysq = es2.enter_context(nc.sbuf_tensor(pfx + "_ysq", [128, 8 * 512], BF16))
        m_sb = es2.enter_context(nc.sbuf_tensor(pfx + "_m", [1, 512], F32))
        m2_sb = es2.enter_context(nc.sbuf_tensor(pfx + "_m2", [1, 512], F32))
        v_sb = es2.enter_context(nc.sbuf_tensor(pfx + "_v", [1, 512], F32))
        l_sb = es2.enter_context(nc.sbuf_tensor(pfx + "_l", [1, 512], F32))
        rstd_sb = es2.enter_context(nc.sbuf_tensor(pfx + "_rstd", [1, 512], F32))
        t_sb = es2.enter_context(nc.sbuf_tensor(pfx + "_t", [128, 512], F32))
        eps_sb = es2.enter_context(nc.sbuf_tensor(pfx + "_eps", [1, 1], F32))
        pingA = es2.enter_context(nc.psum_tensor(pfx + "_pingA", [128, 512], F32))
        pingB = es2.enter_context(nc.psum_tensor(pfx + "_pingB", [128, 512], F32))
        den_ps = es2.enter_context(nc.psum_tensor(pfx + "_denps", [16, 512], F32))
        stat1 = es2.enter_context(nc.psum_tensor(pfx + "_stat1", [1, 512], F32))
        stat2 = es2.enter_context(nc.psum_tensor(pfx + "_stat2", [1, 512], F32))
        bc1 = es2.enter_context(nc.psum_tensor(pfx + "_bc1", [128, 512], F32))
        bc2 = es2.enter_context(nc.psum_tensor(pfx + "_bc2", [128, 512], F32))
        blk = es2.enter_context(nc.Block())
        ping = [pingA, pingB]

        @blk.gpsimd
        def _(g):
            for d in range(DCH):
                sm["sw"].inc(g.dma_start(
                    out=wq_sb[:, d * 1024:(d + 1) * 1024],
                    in_=WqT_d[d * 128:(d + 1) * 128, :]), 16)
                sm["sw"].inc(g.dma_start(
                    out=wo_sb[:, d * 1024:(d + 1) * 1024],
                    in_=WoT_d[d * 128:(d + 1) * 128, :]), 16)
            sm["sw"].inc(g.dma_start(out=bq_sb[:, :], in_=bq_d[:, :]), 16)
            sm["sw"].inc(g.dma_start(out=bo_sb[:, :], in_=bo_d[:, :]), 16)
            for t in range(n_tiles):
                if t > 0:
                    g.wait_ge(sm["mmq"].h, b["mmq"] + 8 * t)
                for d in range(DCH):
                    sm["sxb"].inc(g.dma_start(
                        out=xbf[:, d * 512:(d + 1) * 512],
                        in_=xT[d * 128:(d + 1) * 128, t * 512:(t + 1) * 512]), 16)

        @blk.sync
        def _(s):
            sm["sctx"].inc(s.dma_start(out=ctx_sb[:, :], in_=ctx_in[:, :]), 16)
            sm["sctx"].inc(s.dma_start(out=E_sb[:, :], in_=E_d.ap()), 16)
            for t in range(n_tiles):
                if t > 0:
                    s.wait_ge(sm["str1"].h, b["str1"] + 128 * t)
                for d in range(DCH):
                    sm["sxf"].inc(s.dma_start(
                        out=x_sb[:, d * 512:(d + 1) * 512],
                        in_=xT[d * 128:(d + 1) * 128, t * 512:(t + 1) * 512]), 16)
                for c in range(DCH):
                    s.wait_ge(sm["dveapply"].h, b["dveapply"] + 8 * t + c + 1)
                    sm["str1"].inc(s.dma_start(
                        out=r1_out[c * 128:(c + 1) * 128, t * 512:(t + 1) * 512],
                        in_=x_sb[:, c * 512:(c + 1) * 512]), 16)

        @blk.tensor
        def _(t_):
            t_ = GP(t_)
            t_.wait_ge(sm["sw"].h, b["sw"] + 16 * 18)
            t_.wait_ge(sm["dveinit"].h, b["dveinit"] + 1)
            t_.wait_ge(sm["sctx"].h, b["sctx"] + 32)
            for t in range(n_tiles):
                b_ix = t // 8
                t_.wait_ge(sm["sxb"].h, b["sxb"] + 128 * (t + 1))
                if t > 0:
                    t_.wait_ge(sm["dvey"].h, b["dvey"] + 8 * t)
                t_.on = en >= 1
                for j in range(DCH):
                    gq = 8 * t + j
                    if gq >= 2:
                        t_.wait_ge(sm["actq"].h, b["actq"] + gq - 1)
                    t_.matmul(ping[j % 2][:, :], bq_sb[:, j * 128:(j + 1) * 128],
                              ones512[:, :], start=True, stop=False)
                    for d in range(DCH):
                        ins = t_.matmul(
                            ping[j % 2][:, :],
                            wq_sb[:, d * 1024 + j * 128: d * 1024 + (j + 1) * 128],
                            xbf[:, d * 512:(d + 1) * 512],
                            start=False, stop=(d == DCH - 1))
                    sm["mmq"].inc(ins)
                t_.on = en >= 2
                t_.wait_ge(sm["dveq"].h, b["dveq"] + 8 * t + 8)
                t_.wait_ge(sm["dveden"].h, b["dveden"] + t)
                t_.wait_ge(sm["dvekb"].h, b["dvekb"] + 1 + (1 if t >= 8 else 0))
                for c in range(DCH):
                    ins = t_.matmul(den_ps[:, :], KB_sb[:, c * 16:(c + 1) * 16],
                                    q_sb[:, c * 512:(c + 1) * 512],
                                    start=(c == 0), stop=(c == DCH - 1))
                sm["mmden"].inc(ins)
                t_.on = en >= 3
                t_.wait_ge(sm["actq"].h, b["actq"] + 8 * t + 8)
                t_.wait_ge(sm["dveden"].h, b["dveden"] + t + 1)
                for j in range(DCH):
                    if j >= 2:
                        t_.wait_ge(sm["actr"].h, b["actr"] + 8 * t + j - 1)
                    ins = t_.matmul(ping[j % 2][:, :], E_sb[:, j * 128:(j + 1) * 128],
                                    rbf[:, :], start=True, stop=True)
                    sm["mmr"].inc(ins)
                t_.on = en >= 4
                t_.wait_ge(sm["actr"].h, b["actr"] + 8 * t + 8)
                for j in range(DCH):
                    if j >= 2:
                        t_.wait_ge(sm["dveop"].h, b["dveop"] + 8 * t + j - 1)
                    h0, h1 = 2 * j, 2 * j + 1
                    _, fo0 = ctx_head_off(b_ix, h0)
                    _, fo1 = ctx_head_off(b_ix, h1)
                    t_.matmul(ping[j % 2][0:64, :], ctx_sb[0:64, fo0:fo0 + 64],
                              q_sb[0:64, j * 512:(j + 1) * 512],
                              start=True, stop=True)
                    ins = t_.matmul(ping[j % 2][64:128, :],
                                    ctx_sb[64:128, fo1:fo1 + 64],
                                    q_sb[64:128, j * 512:(j + 1) * 512],
                                    start=True, stop=True)
                    sm["mmnum"].inc(ins)
                t_.on = en >= 5
                t_.wait_ge(sm["dveop"].h, b["dveop"] + 8 * t + 8)
                for d in range(DCH):
                    if d >= 2:
                        t_.wait_ge(sm["dvey"].h, b["dvey"] + 8 * t + d - 1)
                    t_.matmul(ping[d % 2][:, :], bo_sb[:, d * 128:(d + 1) * 128],
                              ones512[:, :], start=True, stop=False)
                    for hv in range(DCH):
                        ins = t_.matmul(
                            ping[d % 2][:, :],
                            wo_sb[:, hv * 1024 + d * 128: hv * 1024 + (d + 1) * 128],
                            op_sb[:, hv * 512:(hv + 1) * 512],
                            start=False, stop=(hv == DCH - 1))
                    sm["mmo"].inc(ins)
                t_.on = en >= 6
                t_.wait_ge(sm["dvecast"].h, b["dvecast"] + 8 * (t + 1))
                t_.wait_ge(sm["actys"].h, b["actys"] + 8 * (t + 1))
                t_.wait_ge(sm["actm2"].h, b["actm2"] + t)
                t_.wait_ge(sm["dvevar"].h, b["dvevar"] + t)
                for c in range(DCH):
                    t_.matmul(stat1[:, :], ones128[:, :],
                              ybf[:, c * 512:(c + 1) * 512],
                              start=(c == 0), stop=(c == DCH - 1))
                for c in range(DCH):
                    ins = t_.matmul(stat2[:, :], ones128[:, :],
                                    ysq[:, c * 512:(c + 1) * 512],
                                    start=(c == 0), stop=(c == DCH - 1))
                sm["mmstat"].inc(ins)
                t_.on = en >= 9
                t_.wait_ge(sm["actrs"].h, b["actrs"] + t + 1)
                t_.wait_ge(sm["dveapply"].h, b["dveapply"] + 8 * t)
                t_.matmul(bc1[:, :], ones1f[:, :], m_sb[:, :], start=True, stop=True)
                ins = t_.matmul(bc2[:, :], ones1f[:, :], rstd_sb[:, :],
                                start=True, stop=True)
                sm["mmbc"].inc(ins)

        @blk.scalar
        def _(s):
            s = GP(s)
            for t in range(n_tiles):
                s.on = en >= 1
                for j in range(DCH):
                    gq = 8 * t + j
                    s.wait_ge(sm["mmq"].h, b["mmq"] + gq + 1)
                    if gq > 0:
                        s.wait_ge(sm["dveq"].h, b["dveq"] + gq)
                    s.activation(e_sb[:, :], ping[j % 2][:, :], AF.Exp)
                    sm["actq"].inc(s.activation(z1_sb[:, :], ping[j % 2][:, :],
                                                AF.Copy, bias=1.0))
                s.on = en >= 3
                s.wait_ge(sm["dveop"].h, b["dveop"] + 8 * t)
                for j in range(DCH):
                    s.wait_ge(sm["mmr"].h, b["mmr"] + 8 * t + j + 1)
                    sm["actr"].inc(s.copy(R_sb[:, j * 512:(j + 1) * 512],
                                          ping[j % 2][:, :]))
                s.on = en >= 6
                s.wait_ge(sm["mmstat"].h, b["mmstat"] + t)
                for c in range(DCH):
                    s.wait_ge(sm["dvey"].h, b["dvey"] + 8 * t + c + 1)
                    sm["actys"].inc(s.activation(ysq[:, c * 512:(c + 1) * 512],
                                                 y_sb[:, c * 512:(c + 1) * 512],
                                                 AF.Square))
                s.on = en >= 7
                s.wait_ge(sm["mmstat"].h, b["mmstat"] + t + 1)
                s.activation(m_sb[:, :], stat1[:, :], AF.Copy, scale=CINV)
                sm["actm2"].inc(s.activation(m2_sb[:, :], stat1[:, :], AF.Square,
                                             scale=CINV))
                s.on = en >= 8
                s.wait_ge(sm["dvevar"].h, b["dvevar"] + t + 1)
                s.activation(l_sb[:, :], v_sb[:, :], AF.Ln, bias=eps_sb[:, :])
                sm["actrs"].inc(s.activation(rstd_sb[:, :], l_sb[:, :], AF.Exp,
                                             scale=-0.5))

        @blk.vector
        def _(v):
            v = GP(v)
            v.memset(ones1f[:, :], 1.0)
            v.memset(ones128[:, :], 1.0)
            v.memset(ones512[:, :], 1.0)
            v.memset(eps_sb[:, :], LN_EPS)
            sm["dveinit"].inc(v.memset(KB_sb[:, :], 0.0))
            v.wait_ge(sm["sctx"].h, b["sctx"] + 32)
            for h in range(H):
                pb, fo = ctx_head_off(0, h)
                ins = v.tensor_copy(
                    KB_sb[pb:pb + 64, (h // 2) * 16 + h:(h // 2) * 16 + h + 1],
                    ctx_sb[pb:pb + 64, fo + 64:fo + 65])
            sm["dvekb"].inc(ins)
            for t in range(n_tiles):
                if t == 8:
                    v.wait_ge(sm["mmden"].h, b["mmden"] + 8)
                    for h in range(H):
                        pb, fo = ctx_head_off(1, h)
                        ins = v.tensor_copy(
                            KB_sb[pb:pb + 64, (h // 2) * 16 + h:(h // 2) * 16 + h + 1],
                            ctx_sb[pb:pb + 64, fo + 64:fo + 65])
                    sm["dvekb"].inc(ins)
                v.on = en >= 1
                v.wait_ge(sm["mmnum"].h, b["mmnum"] + 8 * t)
                for j in range(DCH):
                    v.wait_ge(sm["actq"].h, b["actq"] + 8 * t + j + 1)
                    sm["dveq"].inc(v.scalar_tensor_tensor(
                        q_sb[:, j * 512:(j + 1) * 512], e_sb[:, :], 1.0,
                        z1_sb[:, :], ALU.min, ALU.max))
                v.on = en >= 2
                v.wait_ge(sm["mmden"].h, b["mmden"] + t + 1)
                v.tensor_scalar(den_sb[:, :], den_ps[:, :], ATTN_EPS, None,
                                ALU.add)
                v.reciprocal(rec_sb[:, :], den_sb[:, :])
                sm["dveden"].inc(v.tensor_copy(rbf[:, :], rec_sb[:, :]))
                v.on = en >= 4
                v.wait_ge(sm["mmo"].h, b["mmo"] + 8 * t)
                for j in range(DCH):
                    v.wait_ge(sm["mmnum"].h, b["mmnum"] + 8 * t + j + 1)
                    v.wait_ge(sm["actr"].h, b["actr"] + 8 * t + j + 1)
                    sm["dveop"].inc(v.tensor_mul(op_sb[:, j * 512:(j + 1) * 512],
                                                 ping[j % 2][:, :],
                                                 R_sb[:, j * 512:(j + 1) * 512]))
                v.on = en >= 5
                v.wait_ge(sm["actys"].h, b["actys"] + 8 * t)
                v.wait_ge(sm["sxf"].h, b["sxf"] + 128 * (t + 1))
                for d in range(DCH):
                    v.wait_ge(sm["mmo"].h, b["mmo"] + 8 * t + d + 1)
                    v.tensor_copy(t_sb[0:1, 0:64], t_sb[0:1, 64:128])
                    sm["dvey"].inc(v.tensor_add(y_sb[:, d * 512:(d + 1) * 512],
                                                ping[d % 2][:, :],
                                                x_sb[:, d * 512:(d + 1) * 512]))
                v.on = en >= 6
                v.wait_ge(sm["mmstat"].h, b["mmstat"] + t)
                for c in range(DCH):
                    sm["dvecast"].inc(v.tensor_copy(ybf[:, c * 512:(c + 1) * 512],
                                                    y_sb[:, c * 512:(c + 1) * 512]))
                v.on = en >= 7
                v.wait_ge(sm["mmstat"].h, b["mmstat"] + t + 1)
                v.wait_ge(sm["actm2"].h, b["actm2"] + t + 1)
                sm["dvevar"].inc(v.scalar_tensor_tensor(
                    v_sb[:, :], stat2[:, :], CINV, m2_sb[:, :], ALU.mult,
                    ALU.subtract))
                v.on = en >= 10
                v.wait_ge(sm["mmbc"].h, b["mmbc"] + t + 1)
                for c in range(DCH):
                    v.tensor_sub(t_sb[:, :], y_sb[:, c * 512:(c + 1) * 512],
                                 bc1[:, :])
                    sm["dveapply"].inc(v.tensor_mul(x_sb[:, c * 512:(c + 1) * 512],
                                                    t_sb[:, :], bc2[:, :]))

        @blk.sync
        def _(s):
            s.wait_ge(sm["str1"].h, b["str1"] + 128 * n_tiles)



FFN_A_SEMS = ["qld", "qred", "qbn", "qthr", "qmmv", "qact", "qw",
              "sxb", "mmh", "acth", "sth", "qlde", "qldo"] + [
              f"sth{s}" for s in range(4)]
FFN_B_SEMS = ["qld", "qred", "qbn", "qthr", "qmmv", "qact", "qw",
              "sh", "sxf", "mmo", "dvey", "dvecast", "actys", "mmstat",
              "actm2", "dvevar", "actrs", "mmbc", "dveapply", "strout",
              "dveinit", "qlde", "qldo"]

CINV = 1.0 / 1024.0


def tern_plan(n_rows, n_cols, mode):
    n_el = n_rows * 128 * n_cols
    c = 0.5 / n_el
    if mode == "ternary":
        return c, 5e-9, ALU.max
    return c, 5e-6, ALU.add


class Tern:

    def __init__(self, nc, es2, sm, b, WT_d, scr_d, Wt_sb, nch, cols, mode,
                 ones1f, pfx):
        self.nc, self.sm, self.b = nc, sm, b
        self.WT_d, self.scr_d, self.Wt_sb = WT_d, scr_d, Wt_sb
        self.nch, self.cols, self.mode = nch, cols, mode
        self.ones1f = ones1f
        self.cmul, self.cofs, self.cop = tern_plan(nch, cols, mode)
        self.wbuf = [
            es2.enter_context(nc.sbuf_tensor(pfx + "_wb0", [128, cols], F32)),
            es2.enter_context(nc.sbuf_tensor(pfx + "_wb1", [128, cols], F32)),
        ]
        self.a_sb = es2.enter_context(nc.sbuf_tensor(pfx + "_a", [128, cols], F32))
        self.sg_sb = es2.enter_context(nc.sbuf_tensor(pfx + "_sg", [128, cols], BF16))
        self.part = es2.enter_context(nc.sbuf_tensor(pfx + "_part", [128, nch + 1], F32))
        self.prow = es2.enter_context(nc.sbuf_tensor(pfx + "_prow", [1, 128], F32))
        self.tot = es2.enter_context(nc.sbuf_tensor(pfx + "_tot", [1, 1], F32))
        self.thr = es2.enter_context(nc.sbuf_tensor(pfx + "_thr", [1, 1], F32))
        self.thrv = es2.enter_context(nc.sbuf_tensor(pfx + "_thrv", [128, 1], F32))
        self.tv_ps = es2.enter_context(nc.psum_tensor(pfx + "_tvps", [128, 1], F32))

    def gpsimd(self, g):
        sm, b = self.sm, self.b
        for d in range(self.nch):
            if d >= 2:
                g.wait_ge(sm["qred"].h, b["qred"] + d - 1)
            sm["qlde" if d % 2 == 0 else "qldo"].inc(g.dma_start(
                out=self.wbuf[d % 2][:, :],
                in_=self.WT_d[d * 128:(d + 1) * 128, :]), 16)
        g.wait_ge(sm["qred"].h, b["qred"] + self.nch + 1)
        sm["qbn"].inc(g.dma_start(
            out=self.scr_d.ap().rearrange("a b -> b a"),
            in_=self.part[:, self.nch:self.nch + 1]), 16)
        g.wait_ge(sm["qbn"].h, b["qbn"] + 16)
        sm["qbn"].inc(g.dma_start(out=self.prow[:, :], in_=self.scr_d.ap()), 16)
        for d in range(self.nch):
            if d >= 2:
                g.wait_ge(sm["qw"].h, b["qw"] + d - 1)
                g.wait_ge(sm["qact"].h, b["qact"] + 1 + d - 1)
            sm["qlde" if d % 2 == 0 else "qldo"].inc(g.dma_start(
                out=self.wbuf[d % 2][:, :],
                in_=self.WT_d[d * 128:(d + 1) * 128, :]), 16)

    def vector(self, v):
        sm, b = self.sm, self.b
        for d in range(self.nch):
            v.wait_ge(sm["qlde" if d % 2 == 0 else "qldo"].h,
                      b["qlde" if d % 2 == 0 else "qldo"] + 16 * (d // 2 + 1))
            sm["qred"].inc(v.tensor_reduce(
                self.part[:, d:d + 1], self.wbuf[d % 2][:, :], AX.X,
                ALU.add, apply_absolute_value=True))
        v.wait_ge(sm["qred"].h, b["qred"] + self.nch)
        sm["qred"].inc(v.tensor_reduce(self.part[:, self.nch:self.nch + 1],
                                       self.part[:, 0:self.nch],
                                       AX.X, ALU.add))
        v.wait_ge(sm["qbn"].h, b["qbn"] + 32)
        sm["qred"].inc(v.tensor_reduce(self.tot[:, :], self.prow[:, :],
                                       AX.X, ALU.add))
        v.wait_ge(sm["qred"].h, b["qred"] + self.nch + 2)
        sm["qthr"].inc(v.tensor_scalar(self.thr[:, :], self.tot[:, :],
                                       self.cmul, self.cofs, ALU.mult, self.cop))
        for d in range(self.nch):
            v.wait_ge(sm["qact"].h, b["qact"] + 1 + d + 1)
            sm["qw"].inc(v.scalar_tensor_tensor(
                self.Wt_sb[:, d * self.cols:(d + 1) * self.cols],
                self.a_sb[:, :], self.thrv[:, :], self.sg_sb[:, :],
                ALU.is_ge, ALU.mult))

    def tensor(self, t_):
        sm, b = self.sm, self.b
        t_.wait_ge(sm["qthr"].h, b["qthr"] + 1)
        sm["qmmv"].inc(t_.matmul(self.tv_ps[:, :], self.ones1f[:, :],
                                 self.thr[:, :], start=True, stop=True))

    def scalar(self, s):
        sm, b = self.sm, self.b
        s.wait_ge(sm["qmmv"].h, b["qmmv"] + 1)
        sm["qact"].inc(s.copy(self.thrv[:, :], self.tv_ps[:, :]))
        for d in range(self.nch):
            pw = "qlde" if d % 2 == 0 else "qldo"
            s.wait_ge(sm[pw].h, b[pw] + 16 * (self.nch // 2 + d // 2 + 1))
            if d > 0:
                s.wait_ge(sm["qw"].h, b["qw"] + d)
            s.activation(self.a_sb[:, :], self.wbuf[d % 2][:, :], AF.Abs)
            sm["qact"].inc(s.activation(self.sg_sb[:, :],
                                        self.wbuf[d % 2][:, :], AF.Sign))


def emit_ffn_a(nc, es, sm, rin, W1T_d, b1_d, scr_d, h_out, act_fn, mode,
               pfx="fa", n_tiles=16, en=99, dbg=None):
    b = {k: s.n for k, s in sm.items()}
    from contextlib import ExitStack
    with ExitStack() as es2:
        W1t = es2.enter_context(nc.sbuf_tensor(pfx + "_w1t", [128, 8 * 4096], BF16))
        b1_sb = es2.enter_context(nc.sbuf_tensor(pfx + "_b1", [1, 4096], BF16))
        ones1f = es2.enter_context(nc.sbuf_tensor(pfx + "_o1f", [1, 128], F32))
        ones512 = es2.enter_context(nc.sbuf_tensor(pfx + "_o512", [1, 512], BF16))
        xbf = es2.enter_context(nc.sbuf_tensor(pfx + "_xbf", [128, 8 * 512], BF16))
        h_sb = es2.enter_context(nc.sbuf_tensor(pfx + "_h", [128, 4 * 512], BF16))
        tern = Tern(nc, es2, sm, b, W1T_d, scr_d, W1t, DCH, 4096, mode,
                    ones1f, pfx + "_t1")
        pingA = es2.enter_context(nc.psum_tensor(pfx + "_pgA", [128, 512], F32))
        pingB = es2.enter_context(nc.psum_tensor(pfx + "_pgB", [128, 512], F32))
        blk = es2.enter_context(nc.Block())
        ping = [pingA, pingB]

        @blk.gpsimd
        def _(g):
            tern.gpsimd(g)
            sm["qld"].inc(g.dma_start(out=b1_sb[:, :], in_=b1_d[:, :]), 16)
            for t in range(n_tiles):
                if t > 0:
                    g.wait_ge(sm["mmh"].h, b["mmh"] + 32 * t)
                for d in range(DCH):
                    sm["sxb"].inc(g.dma_start(
                        out=xbf[:, d * 512:(d + 1) * 512],
                        in_=rin[d * 128:(d + 1) * 128, t * 512:(t + 1) * 512]), 16)

        @blk.vector
        def _(v):
            v.memset(ones1f[:, :], 1.0)
            v.memset(ones512[:, :], 1.0)
            tern.vector(v)

        @blk.tensor
        def _(t_):
            t_ = GP(t_)
            tern.tensor(t_)
            t_.wait_ge(sm["qw"].h, b["qw"] + DCH)
            t_.wait_ge(sm["qld"].h, b["qld"] + 16)
            t_.on = en >= 1
            for t in range(n_tiles):
                t_.wait_ge(sm["sxb"].h, b["sxb"] + 128 * (t + 1))
                for j in range(32):
                    gq = 32 * t + j
                    if gq >= 2:
                        t_.wait_ge(sm["acth"].h, b["acth"] + gq - 1)
                    t_.matmul(ping[j % 2][:, :], b1_sb[:, j * 128:(j + 1) * 128],
                              ones512[:, :], start=True, stop=False)
                    for d in range(DCH):
                        ins = t_.matmul(
                            ping[j % 2][:, :],
                            W1t[:, d * 4096 + j * 128: d * 4096 + (j + 1) * 128],
                            xbf[:, d * 512:(d + 1) * 512],
                            start=False, stop=(d == DCH - 1))
                    sm["mmh"].inc(ins)

        @blk.scalar
        def _(s):
            s = GP(s)
            tern.scalar(s)
            s.on = en >= 1
            for t in range(n_tiles):
                for j in range(32):
                    gq = 32 * t + j
                    s.wait_ge(sm["mmh"].h, b["mmh"] + gq + 1)
                    if gq >= 4:
                        s.wait_ge(sm[f"sth{gq % 4}"].h,
                                  b[f"sth{gq % 4}"] + 16 * (gq // 4))
                    sm["acth"].inc(s.activation(
                        h_sb[:, (gq % 4) * 512:(gq % 4 + 1) * 512],
                        ping[j % 2][:, :], act_fn))

        @blk.sync
        def _(s):
            for t in range(n_tiles):
                for j in range(32):
                    gq = 32 * t + j
                    s.wait_ge(sm["acth"].h, b["acth"] + gq + 1)
                    sm[f"sth{gq % 4}"].inc(s.dma_start(
                        out=h_out[j * 128:(j + 1) * 128, t * 512:(t + 1) * 512],
                        in_=h_sb[:, (gq % 4) * 512:(gq % 4 + 1) * 512]), 16)
            for sl in range(4):
                s.wait_ge(sm[f"sth{sl}"].h, b[f"sth{sl}"] + 16 * 8 * n_tiles)
            if dbg is not None:
                nst = 0
                if "wt" in dbg:
                    s.dma_start(out=dbg["wt"][:, :], in_=W1t[:, :]).then_inc(
                        sm["dump"].h, 16)
                    nst += 16
                if "pr" in dbg:
                    s.dma_start(out=dbg["pr"][0:1, 0:128],
                                in_=tern.prow[:, :]).then_inc(sm["dump"].h, 16)
                    s.dma_start(out=dbg["pr"][0:1, 128:129],
                                in_=tern.tot[:, :]).then_inc(sm["dump"].h, 16)
                    s.dma_start(out=dbg["pr"][0:1, 129:130],
                                in_=tern.thr[:, :]).then_inc(sm["dump"].h, 16)
                    s.dma_start(out=dbg["pr"][0:1, 130:131],
                                in_=tern.thrv[0:1, 0:1]).then_inc(sm["dump"].h, 16)
                    nst += 64
                s.wait_ge(sm["dump"].h, nst)


def emit_ffn_b(nc, es, sm, rin, h_in, W2T_d, b2_d, scr_d, rout, mode,
               pfx="fb", n_tiles=16, en=99):
    b = {k: s.n for k, s in sm.items()}
    from contextlib import ExitStack
    with ExitStack() as es2:
        W2t = es2.enter_context(nc.sbuf_tensor(pfx + "_w2t", [128, 32 * 1024], BF16))
        b2_sb = es2.enter_context(nc.sbuf_tensor(pfx + "_b2", [1, 1024], BF16))
        ones1f = es2.enter_context(nc.sbuf_tensor(pfx + "_o1f", [1, 128], F32))
        ones128 = es2.enter_context(nc.sbuf_tensor(pfx + "_o128", [128, 1], BF16))
        ones512 = es2.enter_context(nc.sbuf_tensor(pfx + "_o512", [1, 512], BF16))
        h_sb = es2.enter_context(nc.sbuf_tensor(pfx + "_h", [128, 32 * 512], BF16))
        x_sb = es2.enter_context(nc.sbuf_tensor(pfx + "_x", [128, 8 * 512], F32))
        y_sb = es2.enter_context(nc.sbuf_tensor(pfx + "_y", [128, 8 * 512], F32))
        ybf = es2.enter_context(nc.sbuf_tensor(pfx + "_ybf", [128, 8 * 512], BF16))
        ysq = es2.enter_context(nc.sbuf_tensor(pfx + "_ysq", [128, 8 * 512], BF16))
        m_sb = es2.enter_context(nc.sbuf_tensor(pfx + "_m", [1, 512], F32))
        m2_sb = es2.enter_context(nc.sbuf_tensor(pfx + "_m2", [1, 512], F32))
        v_sb = es2.enter_context(nc.sbuf_tensor(pfx + "_v", [1, 512], F32))
        l_sb = es2.enter_context(nc.sbuf_tensor(pfx + "_l", [1, 512], F32))
        rstd_sb = es2.enter_context(nc.sbuf_tensor(pfx + "_rstd", [1, 512], F32))
        t_sb = es2.enter_context(nc.sbuf_tensor(pfx + "_t", [128, 512], F32))
        eps_sb = es2.enter_context(nc.sbuf_tensor(pfx + "_eps", [1, 1], F32))
        tern = Tern(nc, es2, sm, b, W2T_d, scr_d, W2t, 32, 1024, mode,
                    ones1f, pfx + "_t2")
        pingA = es2.enter_context(nc.psum_tensor(pfx + "_pgA", [128, 512], F32))
        pingB = es2.enter_context(nc.psum_tensor(pfx + "_pgB", [128, 512], F32))
        stat1 = es2.enter_context(nc.psum_tensor(pfx + "_st1", [1, 512], F32))
        stat2 = es2.enter_context(nc.psum_tensor(pfx + "_st2", [1, 512], F32))
        bc1 = es2.enter_context(nc.psum_tensor(pfx + "_bc1", [128, 512], F32))
        bc2 = es2.enter_context(nc.psum_tensor(pfx + "_bc2", [128, 512], F32))
        blk = es2.enter_context(nc.Block())
        ping = [pingA, pingB]

        @blk.gpsimd
        def _(g):
            tern.gpsimd(g)
            sm["qld"].inc(g.dma_start(out=b2_sb[:, :], in_=b2_d[:, :]), 16)

        @blk.sync
        def _(s):
            for t in range(n_tiles):
                if t > 0:
                    s.wait_ge(sm["mmo"].h, b["mmo"] + 8 * t)
                for j in range(32):
                    sm["sh"].inc(s.dma_start(
                        out=h_sb[:, j * 512:(j + 1) * 512],
                        in_=h_in[j * 128:(j + 1) * 128, t * 512:(t + 1) * 512]), 16)
                if t > 0:
                    s.wait_ge(sm["strout"].h, b["strout"] + 128 * t)
                for d in range(DCH):
                    sm["sxf"].inc(s.dma_start(
                        out=x_sb[:, d * 512:(d + 1) * 512],
                        in_=rin[d * 128:(d + 1) * 128, t * 512:(t + 1) * 512]), 16)
                for c in range(DCH):
                    s.wait_ge(sm["dveapply"].h, b["dveapply"] + 8 * t + c + 1)
                    sm["strout"].inc(s.dma_start(
                        out=rout[c * 128:(c + 1) * 128, t * 512:(t + 1) * 512],
                        in_=x_sb[:, c * 512:(c + 1) * 512]), 16)
            s.wait_ge(sm["strout"].h, b["strout"] + 128 * n_tiles)

        @blk.tensor
        def _(t_):
            t_ = GP(t_)
            tern.tensor(t_)
            t_.wait_ge(sm["qw"].h, b["qw"] + 32)
            t_.wait_ge(sm["qld"].h, b["qld"] + 16)
            t_.wait_ge(sm["dveinit"].h, b["dveinit"] + 1)
            t_.on = en >= 1
            for t in range(n_tiles):
                t_.wait_ge(sm["sh"].h, b["sh"] + 512 * (t + 1))
                for dd in range(DCH):
                    if dd >= 2:
                        t_.wait_ge(sm["dvey"].h, b["dvey"] + 8 * t + dd - 1)
                    t_.matmul(ping[dd % 2][:, :],
                              b2_sb[:, dd * 128:(dd + 1) * 128],
                              ones512[:, :], start=True, stop=False)
                    for f in range(32):
                        ins = t_.matmul(
                            ping[dd % 2][:, :],
                            W2t[:, f * 1024 + dd * 128: f * 1024 + (dd + 1) * 128],
                            h_sb[:, f * 512:(f + 1) * 512],
                            start=False, stop=(f == 31))
                    sm["mmo"].inc(ins)
                t_.on = en >= 2
                t_.wait_ge(sm["dvecast"].h, b["dvecast"] + 8 * (t + 1))
                t_.wait_ge(sm["actys"].h, b["actys"] + 8 * (t + 1))
                t_.wait_ge(sm["actm2"].h, b["actm2"] + t)
                t_.wait_ge(sm["dvevar"].h, b["dvevar"] + t)
                for c in range(DCH):
                    t_.matmul(stat1[:, :], ones128[:, :],
                              ybf[:, c * 512:(c + 1) * 512],
                              start=(c == 0), stop=(c == DCH - 1))
                for c in range(DCH):
                    ins = t_.matmul(stat2[:, :], ones128[:, :],
                                    ysq[:, c * 512:(c + 1) * 512],
                                    start=(c == 0), stop=(c == DCH - 1))
                sm["mmstat"].inc(ins)
                t_.on = en >= 3
                t_.wait_ge(sm["actrs"].h, b["actrs"] + t + 1)
                t_.wait_ge(sm["dveapply"].h, b["dveapply"] + 8 * t)
                t_.matmul(bc1[:, :], ones1f[:, :], m_sb[:, :], start=True,
                          stop=True)
                ins = t_.matmul(bc2[:, :], ones1f[:, :], rstd_sb[:, :],
                                start=True, stop=True)
                sm["mmbc"].inc(ins)

        @blk.scalar
        def _(s):
            s = GP(s)
            tern.scalar(s)
            s.on = en >= 2
            for t in range(n_tiles):
                s.wait_ge(sm["mmstat"].h, b["mmstat"] + t)
                for c in range(DCH):
                    s.wait_ge(sm["dvey"].h, b["dvey"] + 8 * t + c + 1)
                    sm["actys"].inc(s.activation(ysq[:, c * 512:(c + 1) * 512],
                                                 y_sb[:, c * 512:(c + 1) * 512],
                                                 AF.Square))
                s.wait_ge(sm["mmstat"].h, b["mmstat"] + t + 1)
                s.activation(m_sb[:, :], stat1[:, :], AF.Copy, scale=CINV)
                sm["actm2"].inc(s.activation(m2_sb[:, :], stat1[:, :],
                                             AF.Square, scale=CINV))
                s.on = en >= 3
                s.wait_ge(sm["dvevar"].h, b["dvevar"] + t + 1)
                s.activation(l_sb[:, :], v_sb[:, :], AF.Ln, bias=eps_sb[:, :])
                sm["actrs"].inc(s.activation(rstd_sb[:, :], l_sb[:, :], AF.Exp,
                                             scale=-0.5))
                s.on = en >= 2

        @blk.vector
        def _(v):
            v = GP(v)
            v.memset(ones1f[:, :], 1.0)
            v.memset(ones128[:, :], 1.0)
            v.memset(ones512[:, :], 1.0)
            sm["dveinit"].inc(v.memset(eps_sb[:, :], LN_EPS))
            tern.vector(v)
            v.on = en >= 2
            for t in range(n_tiles):
                v.wait_ge(sm["actys"].h, b["actys"] + 8 * t)
                v.wait_ge(sm["sxf"].h, b["sxf"] + 128 * (t + 1))
                for dd in range(DCH):
                    v.wait_ge(sm["mmo"].h, b["mmo"] + 8 * t + dd + 1)
                    v.tensor_copy(t_sb[0:1, 0:64], t_sb[0:1, 64:128])
                    sm["dvey"].inc(v.tensor_add(y_sb[:, dd * 512:(dd + 1) * 512],
                                                ping[dd % 2][:, :],
                                                x_sb[:, dd * 512:(dd + 1) * 512]))
                v.wait_ge(sm["mmstat"].h, b["mmstat"] + t)
                for c in range(DCH):
                    sm["dvecast"].inc(v.tensor_copy(ybf[:, c * 512:(c + 1) * 512],
                                                    y_sb[:, c * 512:(c + 1) * 512]))
                v.wait_ge(sm["mmstat"].h, b["mmstat"] + t + 1)
                v.wait_ge(sm["actm2"].h, b["actm2"] + t + 1)
                sm["dvevar"].inc(v.scalar_tensor_tensor(
                    v_sb[:, :], stat2[:, :], CINV, m2_sb[:, :], ALU.mult,
                    ALU.subtract))
                v.on = en >= 3
                v.wait_ge(sm["mmbc"].h, b["mmbc"] + t + 1)
                for c in range(DCH):
                    v.tensor_sub(t_sb[:, :], y_sb[:, c * 512:(c + 1) * 512],
                                 bc1[:, :])
                    sm["dveapply"].inc(v.tensor_mul(x_sb[:, c * 512:(c + 1) * 512],
                                                    t_sb[:, :], bc2[:, :]))
                v.on = en >= 2



P3A_SEMS = ["sw", "sxb", "mmu", "actu", "stu"] + [f"stu{s}" for s in range(4)]
P3B_SEMS = ["sw", "su", "sxf", "acth2", "dveh", "mmo", "dvey", "dvedl",
            "actsse", "dvesse", "mmsse", "actcc", "ccg", "scc", "ccd",
            "dvedec", "mmsv", "actsv", "dvecast", "actys", "mmstat",
            "actm2", "dvevar", "actrs", "mmbc", "dveapply", "strout",
            "dveinit"] + [f"su{s}" for s in range(8)]

CINV = 1.0 / 1024.0
LOSS_C = -0.001 / 8388608.0


def emit_p3a(nc, es, sm, rin, w0_d, w2_d, u_out, t_out, pfx="pa",
             n_tiles=16, en=99):
    b = {k: s.n for k, s in sm.items()}
    from contextlib import ExitStack
    with ExitStack() as es2:
        w0_sb = es2.enter_context(nc.sbuf_tensor(pfx + "_w0", [128, 8 * 4096], BF16))
        w2_sb = es2.enter_context(nc.sbuf_tensor(pfx + "_w2", [128, 8 * 4096], BF16))
        xbf = es2.enter_context(nc.sbuf_tensor(pfx + "_xbf", [128, 8 * 512], BF16))
        u_sb = es2.enter_context(nc.sbuf_tensor(pfx + "_u", [128, 4 * 512], BF16))
        pingA = es2.enter_context(nc.psum_tensor(pfx + "_pgA", [128, 512], F32))
        pingB = es2.enter_context(nc.psum_tensor(pfx + "_pgB", [128, 512], F32))
        blk = es2.enter_context(nc.Block())
        ping = [pingA, pingB]

        @blk.gpsimd
        def _(g):
            for d in range(DCH):
                sm["sw"].inc(g.dma_start(
                    out=w0_sb[:, d * 4096:(d + 1) * 4096],
                    in_=w0_d[d * 128:(d + 1) * 128, :]), 16)
                sm["sw"].inc(g.dma_start(
                    out=w2_sb[:, d * 4096:(d + 1) * 4096],
                    in_=w2_d[d * 128:(d + 1) * 128, :]), 16)
            for t in range(n_tiles):
                if t > 0:
                    g.wait_ge(sm["mmu"].h, b["mmu"] + 64 * t)
                for d in range(DCH):
                    sm["sxb"].inc(g.dma_start(
                        out=xbf[:, d * 512:(d + 1) * 512],
                        in_=rin[d * 128:(d + 1) * 128, t * 512:(t + 1) * 512]), 16)

        @blk.tensor
        def _(t_):
            t_ = GP(t_)
            t_.wait_ge(sm["sw"].h, b["sw"] + 16 * 16)
            t_.on = en >= 1
            for t in range(n_tiles):
                t_.wait_ge(sm["sxb"].h, b["sxb"] + 128 * (t + 1))
                for gq0 in range(64):
                    gq = 64 * t + gq0
                    w_sb = w0_sb if gq0 < 32 else w2_sb
                    j = gq0 % 32
                    if gq >= 2:
                        t_.wait_ge(sm["actu"].h, b["actu"] + gq - 1)
                    for d in range(DCH):
                        ins = t_.matmul(
                            ping[gq % 2][:, :],
                            w_sb[:, d * 4096 + j * 128: d * 4096 + (j + 1) * 128],
                            xbf[:, d * 512:(d + 1) * 512],
                            start=(d == 0), stop=(d == DCH - 1))
                    sm["mmu"].inc(ins)

        @blk.scalar
        def _(s):
            s = GP(s)
            s.on = en >= 1
            for t in range(n_tiles):
                for gq0 in range(64):
                    gq = 64 * t + gq0
                    s.wait_ge(sm["mmu"].h, b["mmu"] + gq + 1)
                    if gq >= 4:
                        s.wait_ge(sm[f"stu{gq % 4}"].h,
                                  b[f"stu{gq % 4}"] + 16 * (gq // 4))
                    sm["actu"].inc(s.copy(
                        u_sb[:, (gq % 4) * 512:(gq % 4 + 1) * 512],
                        ping[gq % 2][:, :]))

        @blk.sync
        def _(s):
            for t in range(n_tiles):
                for gq0 in range(64):
                    gq = 64 * t + gq0
                    out_d = u_out if gq0 < 32 else t_out
                    j = gq0 % 32
                    s.wait_ge(sm["actu"].h, b["actu"] + gq + 1)
                    sm[f"stu{gq % 4}"].inc(s.dma_start(
                        out=out_d[j * 128:(j + 1) * 128, t * 512:(t + 1) * 512],
                        in_=u_sb[:, (gq % 4) * 512:(gq % 4 + 1) * 512]), 16)
            for sl in range(4):
                s.wait_ge(sm[f"stu{sl}"].h, b[f"stu{sl}"] + 16 * 16 * n_tiles)


def emit_p3b(nc, es, sm, rin, u_in, t_in, w1_d, cc_in, cc_out, rout,
             pfx="pb", n_chunks=8, en=99, dbg=None):
    b = {k: s.n for k, s in sm.items()}
    from contextlib import ExitStack
    with ExitStack() as es2:
        w1_sb = es2.enter_context(nc.sbuf_tensor(pfx + "_w1", [128, 32 * 1024], BF16))
        ones1f = es2.enter_context(nc.sbuf_tensor(pfx + "_o1f", [1, 128], F32))
        ones128f = es2.enter_context(nc.sbuf_tensor(pfx + "_o128f", [128, 1], F32))
        ones128 = es2.enter_context(nc.sbuf_tensor(pfx + "_o128", [128, 1], BF16))
        u_sb = es2.enter_context(nc.sbuf_tensor(pfx + "_u", [128, 8 * 512], BF16))
        t_sb = es2.enter_context(nc.sbuf_tensor(pfx + "_tt", [128, 8 * 512], BF16))
        a_buf = es2.enter_context(nc.sbuf_tensor(pfx + "_ab", [128, 2 * 512], BF16))
        st_sb = es2.enter_context(nc.sbuf_tensor(pfx + "_st", [128, 512], BF16))
        h_sb = es2.enter_context(nc.sbuf_tensor(pfx + "_h", [128, 32 * 512], BF16))
        x_sb = es2.enter_context(nc.sbuf_tensor(pfx + "_x", [128, 8 * 512], F32))
        y_sb = es2.enter_context(nc.sbuf_tensor(pfx + "_y", [128, 8 * 512], F32))
        ybf = es2.enter_context(nc.sbuf_tensor(pfx + "_ybf", [128, 8 * 512], BF16))
        ysq = es2.enter_context(nc.sbuf_tensor(pfx + "_ysq", [128, 8 * 512], BF16))
        dl_sb = es2.enter_context(nc.sbuf_tensor(pfx + "_dl", [128, 512], F32))
        junk = es2.enter_context(nc.sbuf_tensor(pfx + "_junk", [128, 512], F32))
        ssec = es2.enter_context(nc.sbuf_tensor(pfx + "_ssec", [128, 17], F32))
        ccrow = es2.enter_context(nc.sbuf_tensor(pfx + "_ccrow", [1, 8], F32))
        totrow = es2.enter_context(nc.sbuf_tensor(pfx + "_totrow", [1, 8], F32))
        dec_sb = es2.enter_context(nc.sbuf_tensor(pfx + "_dec", [1, 1], F32))
        s_buf = es2.enter_context(nc.sbuf_tensor(pfx + "_sbf", [1, 2], F32))
        s2_sb = es2.enter_context(nc.sbuf_tensor(pfx + "_s2", [1, 1], F32))
        s_vec = es2.enter_context(nc.sbuf_tensor(pfx + "_sv", [128, 1], F32))
        s2_vec = es2.enter_context(nc.sbuf_tensor(pfx + "_s2v", [128, 1], F32))
        m_sb = es2.enter_context(nc.sbuf_tensor(pfx + "_m", [1, 512], F32))
        m2_sb = es2.enter_context(nc.sbuf_tensor(pfx + "_m2", [1, 512], F32))
        v_sb = es2.enter_context(nc.sbuf_tensor(pfx + "_v", [1, 512], F32))
        l_sb = es2.enter_context(nc.sbuf_tensor(pfx + "_l", [1, 512], F32))
        rstd_sb = es2.enter_context(nc.sbuf_tensor(pfx + "_rstd", [1, 512], F32))
        tt_sb = es2.enter_context(nc.sbuf_tensor(pfx + "_t", [128, 512], F32))
        eps_sb = es2.enter_context(nc.sbuf_tensor(pfx + "_eps", [1, 1], F32))
        junkr = es2.enter_context(nc.sbuf_tensor(pfx + "_jr", [1, 128], F32))
        pingA = es2.enter_context(nc.psum_tensor(pfx + "_pgA", [128, 512], F32))
        pingB = es2.enter_context(nc.psum_tensor(pfx + "_pgB", [128, 512], F32))
        stat1 = es2.enter_context(nc.psum_tensor(pfx + "_st1", [1, 512], F32))
        stat2 = es2.enter_context(nc.psum_tensor(pfx + "_st2", [1, 512], F32))
        bc1 = es2.enter_context(nc.psum_tensor(pfx + "_bc1", [128, 512], F32))
        bc2 = es2.enter_context(nc.psum_tensor(pfx + "_bc2", [128, 512], F32))
        sse_ps = es2.enter_context(nc.psum_tensor(pfx + "_sse", [1, 1], F32))
        sv_ps = es2.enter_context(nc.psum_tensor(pfx + "_svp", [128, 2], F32))
        blk = es2.enter_context(nc.Block())
        ping = [pingA, pingB]
        n_tiles = 2 * n_chunks

        def col(c, bb):
            return bb * 4096 + c * 512

        @blk.gpsimd
        def _(g):
            for f in range(32):
                sm["sw"].inc(g.dma_start(
                    out=w1_sb[:, f * 1024:(f + 1) * 1024],
                    in_=w1_d[f * 128:(f + 1) * 128, :]), 16)
            for c in range(n_chunks):
                g.wait_ge(sm["actcc"].h, b["actcc"] + c + 1)
                sm["ccg"].inc(g.dma_start(out=cc_in[c].ap(), in_=ccrow[:, :]), 16)
                g.wait_ge(sm["ccg"].h, b["ccg"] + 16 * (c + 1))
                sm["scc"].inc(g.collective_compute(
                    "AllReduce", ALU.add,
                    replica_groups=[list(range(8))],
                    ins=[cc_in[c].ap()],
                    outs=[cc_out[c].ap()]), 1)
                g.wait_ge(sm["scc"].h,
                          b["scc"] + c + 1 + (16 if dbg is not None and "h0" in dbg else 0))
                sm["ccd"].inc(g.dma_start(out=totrow[:, :], in_=cc_out[c].ap()), 16)
                if dbg is not None:
                    g.wait_ge(sm["dvedec"].h, b["dvedec"] + 3 * (c + 1))
                    sm["ccg"].inc(g.dma_start(
                        out=dbg["tots"][c:c + 1, 0:8], in_=totrow[:, :]), 16)
                    sm["ccg"].inc(g.dma_start(
                        out=dbg["svals"][c:c + 1, 0:2], in_=s_buf[:, :]), 16)

        @blk.sync
        def _(s):
            for c in range(n_chunks):
                for bb in range(2):
                    g_t = 2 * c + bb
                    for f in range(32):
                        G = 32 * g_t + f
                        if G >= 8:
                            s.wait_ge(sm["dveh"].h, b["dveh"] + G - 7)
                        sl = f % 8
                        sm[f"su{sl}"].inc(s.dma_start(
                            out=u_sb[:, sl * 512:(sl + 1) * 512],
                            in_=u_in[f * 128:(f + 1) * 128,
                                     col(c, bb):col(c, bb) + 512]), 16)
                        sm[f"su{sl}"].inc(s.dma_start(
                            out=t_sb[:, sl * 512:(sl + 1) * 512],
                            in_=t_in[f * 128:(f + 1) * 128,
                                     col(c, bb):col(c, bb) + 512]), 16)
                    if g_t > 0:
                        s.wait_ge(sm["strout"].h, b["strout"] + 128 * g_t)
                    for d in range(DCH):
                        sm["sxf"].inc(s.dma_start(
                            out=x_sb[:, d * 512:(d + 1) * 512],
                            in_=rin[d * 128:(d + 1) * 128,
                                    col(c, bb):col(c, bb) + 512]), 16)
                    if dbg is not None and g_t == 0 and "x0" in dbg:
                        s.dma_start(out=dbg["x0"].ap(),
                                    in_=x_sb[:, :]).then_inc(sm["ccg"].h, 16)
                    if dbg is not None and g_t == 0 and "h0" in dbg:
                        s.wait_ge(sm["mmo"].h, b["mmo"] + 8)
                        s.dma_start(out=dbg["h0"].ap(),
                                    in_=h_sb[:, :]).then_inc(sm["scc"].h, 16)
                    for cc in range(DCH):
                        s.wait_ge(sm["dveapply"].h,
                                  b["dveapply"] + 8 * g_t + cc + 1)
                        sm["strout"].inc(s.dma_start(
                            out=rout[cc * 128:(cc + 1) * 128,
                                     col(c, bb):col(c, bb) + 512],
                            in_=x_sb[:, cc * 512:(cc + 1) * 512]), 16)
            s.wait_ge(sm["strout"].h, b["strout"] + 128 * n_tiles)

        @blk.tensor
        def _(t_):
            t_ = GP(t_)
            t_.wait_ge(sm["sw"].h, b["sw"] + 16 * 32)
            t_.wait_ge(sm["dveinit"].h, b["dveinit"] + 1)
            t_.on = en >= 1
            for c in range(n_chunks):
                for bb in range(2):
                    g_t = 2 * c + bb
                    t_.wait_ge(sm["dveh"].h, b["dveh"] + 32 * (g_t + 1))
                    for dd in range(DCH):
                        if dd >= 2:
                            t_.wait_ge(sm["dvey"].h, b["dvey"] + 8 * g_t + dd - 1)
                        for f in range(32):
                            ins = t_.matmul(
                                ping[dd % 2][:, :],
                                w1_sb[:, f * 1024 + dd * 128: f * 1024 + (dd + 1) * 128],
                                h_sb[:, f * 512:(f + 1) * 512],
                                start=(f == 0), stop=(f == 31))
                        sm["mmo"].inc(ins)
                    t_.on = en >= 2
                    t_.wait_ge(sm["dvecast"].h, b["dvecast"] + 8 * (g_t + 1))
                    t_.wait_ge(sm["actys"].h, b["actys"] + 8 * (g_t + 1))
                    t_.wait_ge(sm["actm2"].h, b["actm2"] + g_t)
                    t_.wait_ge(sm["dvevar"].h, b["dvevar"] + 2 * g_t)
                    for cc in range(DCH):
                        t_.matmul(stat1[:, :], ones128[:, :],
                                  ybf[:, cc * 512:(cc + 1) * 512],
                                  start=(cc == 0), stop=(cc == DCH - 1))
                    for cc in range(DCH):
                        ins = t_.matmul(stat2[:, :], ones128[:, :],
                                        ysq[:, cc * 512:(cc + 1) * 512],
                                        start=(cc == 0), stop=(cc == DCH - 1))
                    sm["mmstat"].inc(ins)
                    t_.wait_ge(sm["actrs"].h, b["actrs"] + g_t + 1)
                    t_.wait_ge(sm["dveapply"].h, b["dveapply"] + 8 * g_t)
                    t_.matmul(bc1[:, :], ones1f[:, :], m_sb[:, :], start=True,
                              stop=True)
                    ins = t_.matmul(bc2[:, :], ones1f[:, :], rstd_sb[:, :],
                                    start=True, stop=True)
                    sm["mmbc"].inc(ins)
                    t_.on = en >= 1
                t_.on = en >= 3
                t_.wait_ge(sm["dvesse"].h, b["dvesse"] + 2 * c + 2)
                sm["mmsse"].inc(t_.matmul(sse_ps[:, :], ssec[:, 16:17],
                                          ones128f[:, :], start=True, stop=True))
                t_.wait_ge(sm["dvedec"].h, b["dvedec"] + 4 * (c + 1))
                t_.wait_ge(sm["actsv"].h, b["actsv"] + c)
                t_.matmul(sv_ps[:, 0:1], ones1f[:, :],
                          s_buf[:, (c + 1) % 2:(c + 1) % 2 + 1],
                          start=True, stop=True)
                ins = t_.matmul(sv_ps[:, 1:2], ones1f[:, :], s2_sb[:, :],
                                start=True, stop=True)
                sm["mmsv"].inc(ins)
                t_.on = en >= 1

        @blk.scalar
        def _(s):
            s = GP(s)
            s.on = en >= 1
            for c in range(n_chunks):
                for bb in range(2):
                    g_t = 2 * c + bb
                    if c > 0:
                        s.wait_ge(sm["actsv"].h, b["actsv"] + c)
                    for f in range(32):
                        G = 32 * g_t + f
                        s.wait_ge(sm[f"su{G % 8}"].h,
                                  b[f"su{G % 8}"] + 32 * (G // 8 + 1))
                        if G >= 2:
                            s.wait_ge(sm["dveh"].h, b["dveh"] + G - 1)
                        sm["acth2"].inc(s.activation(
                            a_buf[:, (G % 2) * 512:(G % 2 + 1) * 512],
                            u_sb[:, (f % 8) * 512:(f % 8 + 1) * 512],
                            AF.Silu, scale=s_vec[:, :]))
                    s.on = en >= 2
                    s.wait_ge(sm["mmstat"].h, b["mmstat"] + g_t)
                    for cc in range(DCH):
                        s.wait_ge(sm["dvey"].h, b["dvey"] + 8 * g_t + cc + 1)
                        sm["actys"].inc(s.activation(
                            ysq[:, cc * 512:(cc + 1) * 512],
                            y_sb[:, cc * 512:(cc + 1) * 512], AF.Square))
                    s.on = en >= 2
                    s.wait_ge(sm["mmstat"].h, b["mmstat"] + g_t + 1)
                    s.activation(m_sb[:, :], stat1[:, :], AF.Copy, scale=CINV)
                    s.activation(m2_sb[:, :], stat1[:, :], AF.Square, scale=CINV)
                    sm["actm2"].inc(s.copy(junkr[0:1, 0:64], junkr[0:1, 64:128]))
                    s.wait_ge(sm["dvevar"].h, b["dvevar"] + 2 * g_t + 2)
                    s.copy(junkr[0:1, 0:64], junkr[0:1, 64:128])
                    s.activation(l_sb[:, :], v_sb[:, :], AF.Ln, bias=eps_sb[:, :])
                    s.activation(rstd_sb[:, :], l_sb[:, :], AF.Exp, scale=-0.5)
                    sm["actrs"].inc(s.copy(junkr[0:1, 0:64], junkr[0:1, 64:128]))
                    if dbg is not None and "mr" in dbg:
                        s.dma_start(out=dbg["mr"][g_t:g_t + 1, 0:512],
                                    in_=m_sb[:, :]).then_inc(sm["ccg"].h, 16)
                        s.dma_start(out=dbg["mr"][g_t:g_t + 1, 512:1024],
                                    in_=rstd_sb[:, :]).then_inc(sm["ccg"].h, 16)
                    if dbg is not None and g_t == 0 and "y0" in dbg:
                        s.dma_start(out=dbg["y0"].ap(),
                                    in_=y_sb[:, :]).then_inc(sm["ccg"].h, 16)
                        s.dma_start(out=dbg["ybf0"].ap(),
                                    in_=ybf[:, :]).then_inc(sm["ccg"].h, 16)
                        s.dma_start(out=dbg["ysq0"].ap(),
                                    in_=ysq[:, :]).then_inc(sm["ccg"].h, 16)
                        s.dma_start(out=dbg["v0"].ap(),
                                    in_=v_sb[:, :]).then_inc(sm["ccg"].h, 16)
                    s.on = en >= 1
                s.on = en >= 3
                s.wait_ge(sm["mmsse"].h, b["mmsse"] + c + 1)
                s.wait_ge(sm["ccg"].h, b["ccg"] + 16 * c)
                s.copy(ccrow[0:1, 0:1], sse_ps[:, :])
                sm["actcc"].inc(s.copy(junkr[0:1, 0:64], junkr[0:1, 64:128]))
                s.wait_ge(sm["mmsv"].h, b["mmsv"] + c + 1)
                s.copy(junkr[0:1, 0:64], junkr[0:1, 64:128])
                s.copy(s_vec[:, :], sv_ps[:, 0:1])
                s.copy(s2_vec[:, :], sv_ps[:, 1:2])
                sm["actsv"].inc(s.copy(junkr[0:1, 0:64], junkr[0:1, 64:128]))
                s.on = en >= 1

        @blk.vector
        def _(v):
            v = GP(v)
            v.memset(ones1f[:, :], 1.0)
            v.memset(ones128f[:, :], 1.0)
            v.memset(ones128[:, :], 1.0)
            v.memset(eps_sb[:, :], LN_EPS)
            v.memset(s_buf[:, :], 1.0)
            v.memset(s_vec[:, :], 1.0)
            v.memset(s2_vec[:, :], 1.0)
            sm["dveinit"].inc(v.memset(ccrow[:, :], 0.0))
            for c in range(n_chunks):
                for bb in range(2):
                    g_t = 2 * c + bb
                    v.wait_ge(sm["mmo"].h, b["mmo"] + 8 * g_t)
                    if dbg is not None and "h0" in dbg and g_t == 1:
                        v.wait_ge(sm["scc"].h, b["scc"] + 16)
                    if c > 0:
                        v.wait_ge(sm["actsv"].h, b["actsv"] + c)
                    for f in range(32):
                        G = 32 * g_t + f
                        v.wait_ge(sm[f"su{G % 8}"].h,
                                  b[f"su{G % 8}"] + 32 * (G // 8 + 1))
                        v.tensor_scalar(st_sb[:, :],
                                        t_sb[:, (f % 8) * 512:(f % 8 + 1) * 512],
                                        s2_vec[:, :], None, ALU.mult)
                        v.wait_ge(sm["acth2"].h, b["acth2"] + G + 1)
                        sm["dveh"].inc(v.tensor_mul(
                            h_sb[:, f * 512:(f + 1) * 512],
                            a_buf[:, (G % 2) * 512:(G % 2 + 1) * 512],
                            st_sb[:, :]))
                    v.on = en >= 2
                    v.wait_ge(sm["actys"].h, b["actys"] + 8 * g_t)
                    v.wait_ge(sm["sxf"].h, b["sxf"] + 128 * (g_t + 1))
                    for dd in range(DCH):
                        v.wait_ge(sm["mmo"].h, b["mmo"] + 8 * g_t + dd + 1)
                        v.tensor_copy(junk[0:1, 0:64], junk[0:1, 64:128])
                        sm["dvey"].inc(v.tensor_add(
                            y_sb[:, dd * 512:(dd + 1) * 512],
                            ping[dd % 2][:, :],
                            x_sb[:, dd * 512:(dd + 1) * 512]))
                        v.tensor_sub(dl_sb[:, :], ping[dd % 2][:, :],
                                     x_sb[:, dd * 512:(dd + 1) * 512])
                        v.on = en >= 3
                        sm["dvedl"].inc(v.scalar_tensor_tensor(
                            junk[:, :], dl_sb[:, :], 1.0, dl_sb[:, :],
                            ALU.mult, ALU.mult,
                            accum_out=ssec[:, 8 * bb + dd:8 * bb + dd + 1]))
                        v.on = en >= 2
                    v.wait_ge(sm["mmstat"].h, b["mmstat"] + g_t)
                    for cc in range(DCH):
                        sm["dvecast"].inc(v.tensor_copy(
                            ybf[:, cc * 512:(cc + 1) * 512],
                            y_sb[:, cc * 512:(cc + 1) * 512]))
                    v.wait_ge(sm["mmstat"].h, b["mmstat"] + g_t + 1)
                    v.wait_ge(sm["actm2"].h, b["actm2"] + g_t + 1)
                    v.tensor_copy(junk[0:1, 0:64], junk[0:1, 64:128])
                    sm["dvevar"].inc(v.scalar_tensor_tensor(
                        v_sb[:, :], stat2[:, :], CINV, m2_sb[:, :], ALU.mult,
                        ALU.subtract))
                    v.wait_ge(sm["dvevar"].h, b["dvevar"] + 2 * g_t + 1)
                    sm["dvevar"].inc(v.tensor_copy(junk[0:1, 0:64],
                                                   junk[0:1, 64:128]))
                    v.on = en >= 3
                    v.wait_ge(sm["mmbc"].h, b["mmbc"] + g_t + 1)
                    v.tensor_copy(junk[0:1, 0:64], junk[0:1, 64:128])
                    for cc in range(DCH):
                        v.tensor_sub(tt_sb[:, :], y_sb[:, cc * 512:(cc + 1) * 512],
                                     bc1[:, :])
                        sm["dveapply"].inc(v.tensor_mul(
                            x_sb[:, cc * 512:(cc + 1) * 512], tt_sb[:, :],
                            bc2[:, :]))
                    v.on = en >= 1
                v.on = en >= 3
                v.wait_ge(sm["dvedl"].h, b["dvedl"] + 16 * (c + 1))
                if c > 0:
                    v.wait_ge(sm["mmsse"].h, b["mmsse"] + c)
                sm["dvesse"].inc(v.tensor_reduce(ssec[:, 16:17], ssec[:, 0:16],
                                                 AX.X, ALU.add))
                v.wait_ge(sm["dvesse"].h, b["dvesse"] + 2 * c + 1)
                sm["dvesse"].inc(v.tensor_copy(junk[0:1, 0:64],
                                               junk[0:1, 64:128]))
                v.wait_ge(sm["ccd"].h, b["ccd"] + 16 * (c + 1))
                sm["dvedec"].inc(v.tensor_scalar(dec_sb[:, :],
                                                 totrow[0:1, 0:1], LOSS_C, 1.0,
                                                 ALU.mult, ALU.add))
                v.wait_ge(sm["dvedec"].h, b["dvedec"] + 4 * c + 1)
                sm["dvedec"].inc(v.tensor_mul(
                    s_buf[:, (c + 1) % 2:(c + 1) % 2 + 1],
                    s_buf[:, c % 2:c % 2 + 1], dec_sb[:, :]))
                v.wait_ge(sm["dvedec"].h, b["dvedec"] + 4 * c + 2)
                sm["dvedec"].inc(v.tensor_mul(
                    s2_sb[:, :], s_buf[:, (c + 1) % 2:(c + 1) % 2 + 1],
                    s_buf[:, (c + 1) % 2:(c + 1) % 2 + 1]))
                v.wait_ge(sm["dvedec"].h, b["dvedec"] + 4 * c + 3)
                sm["dvedec"].inc(v.tensor_copy(junk[0:1, 0:64],
                                               junk[0:1, 64:128]))
                v.on = en >= 1


from contextlib import ExitStack
from concourse.bass_utils import run_bass_kernel_spmd




N_CORES = 8


def make_E():
    E = np.zeros((16, 1024), np.float32)
    for h in range(16):
        E[h, h * 64:(h + 1) * 64] = 1.0
    return E.astype(BF)


def prep_wT(W):
    return np.ascontiguousarray(np.asarray(W, dtype=np.float32).T)


def prep_row(v):
    return np.ascontiguousarray(np.asarray(v, dtype=np.float32)[None, :])


def build_p0():
    nc = bass.Bass("TRN2", target_bir_lowering=False)
    xT = nc.dram_tensor("xT", [1024, 8192], F32, kind="ExternalInput")
    WkvT = nc.dram_tensor("WkvT", [1024, 2048], F32, kind="ExternalInput")
    bkv = nc.dram_tensor("bkv", [1, 2048], F32, kind="ExternalInput")
    ctx_out = nc.dram_tensor("ctx", [128, 1040], BF16, kind="ExternalOutput")
    with ExitStack() as es:
        emit_p0(nc, es, xT, WkvT, bkv, ctx_out)
    return nc


def build_p1():
    nc = bass.Bass("TRN2", target_bir_lowering=False)
    xT = nc.dram_tensor("xT", [1024, 8192], F32, kind="ExternalInput")
    ctx_in = nc.dram_tensor("ctx", [128, 1040], BF16, kind="ExternalInput")
    WqT = nc.dram_tensor("WqT", [1024, 1024], F32, kind="ExternalInput")
    bq = nc.dram_tensor("bq", [1, 1024], F32, kind="ExternalInput")
    WoT = nc.dram_tensor("WoT", [1024, 1024], F32, kind="ExternalInput")
    bo = nc.dram_tensor("bo", [1, 1024], F32, kind="ExternalInput")
    r1 = nc.dram_tensor("r1", [1024, 8192], F32, kind="ExternalOutput")
    E_d = nc.inline_tensor(make_E(), "Emat")
    with ExitStack() as es:
        sm = mk_sems(nc, es, P1_SEMS, "p1")
        emit_p1(nc, es, sm, xT, ctx_in, WqT, bq, WoT, bo, E_d, r1)
    return nc


def build_ffn_a(act_fn, mode):
    nc = bass.Bass("TRN2", target_bir_lowering=False)
    rin = nc.dram_tensor("rin", [1024, 8192], F32, kind="ExternalInput")
    scr = nc.dram_tensor("scr", [1, 128], F32, kind="Internal")
    W1T = nc.dram_tensor("W1T", [1024, 4096], F32, kind="ExternalInput")
    b1 = nc.dram_tensor("b1", [1, 4096], F32, kind="ExternalInput")
    h_out = nc.dram_tensor("h", [4096, 8192], BF16, kind="ExternalOutput")
    with ExitStack() as es:
        sm = mk_sems(nc, es, FFN_A_SEMS, "fa")
        emit_ffn_a(nc, es, sm, rin, W1T, b1, scr, h_out, act_fn, mode)
    return nc


def build_ffn_b(mode):
    nc = bass.Bass("TRN2", target_bir_lowering=False)
    rin = nc.dram_tensor("rin", [1024, 8192], F32, kind="ExternalInput")
    scr = nc.dram_tensor("scr", [1, 128], F32, kind="Internal")
    h_in = nc.dram_tensor("h", [4096, 8192], BF16, kind="ExternalInput")
    W2T = nc.dram_tensor("W2T", [4096, 1024], F32, kind="ExternalInput")
    b2 = nc.dram_tensor("b2", [1, 1024], F32, kind="ExternalInput")
    rout = nc.dram_tensor("rout", [1024, 8192], F32, kind="ExternalOutput")
    with ExitStack() as es:
        sm = mk_sems(nc, es, FFN_B_SEMS, "fb")
        emit_ffn_b(nc, es, sm, rin, h_in, W2T, b2, scr, rout, mode)
    return nc


def build_p3a():
    nc = bass.Bass("TRN2", target_bir_lowering=False)
    rin = nc.dram_tensor("rin", [1024, 8192], F32, kind="ExternalInput")
    w0_d = nc.dram_tensor("w0", [1024, 4096], F32, kind="ExternalInput")
    w2_d = nc.dram_tensor("w2", [1024, 4096], F32, kind="ExternalInput")
    u_out = nc.dram_tensor("u", [4096, 8192], BF16, kind="ExternalOutput")
    t_out = nc.dram_tensor("t", [4096, 8192], BF16, kind="ExternalOutput")
    with ExitStack() as es:
        sm = mk_sems(nc, es, P3A_SEMS, "pa")
        emit_p3a(nc, es, sm, rin, w0_d, w2_d, u_out, t_out)
    return nc


def build_p3b():
    nc = bass.Bass("TRN2", target_bir_lowering=False)
    rin = nc.dram_tensor("rin", [1024, 8192], F32, kind="ExternalInput")
    u_in = nc.dram_tensor("u", [4096, 8192], BF16, kind="ExternalInput")
    t_in = nc.dram_tensor("t", [4096, 8192], BF16, kind="ExternalInput")
    w1_d = nc.dram_tensor("w1", [4096, 1024], F32, kind="ExternalInput")
    rout = nc.dram_tensor("rout", [1024, 8192], F32, kind="ExternalOutput")
    cc_in = [nc.dram_tensor(f"cci{c}", [1, 8], F32, kind="Internal")
             for c in range(8)]
    cc_out = [nc.dram_tensor(f"cco{c}", [1, 8], F32, kind="Internal")
              for c in range(8)]
    with ExitStack() as es:
        sm = mk_sems(nc, es, P3B_SEMS, "pb")
        emit_p3b(nc, es, sm, rin, u_in, t_in, w1_d, cc_in, cc_out, rout)
    return nc


def fused_sem_names():
    names = []
    for lst in (P1_SEMS, FFN_A_SEMS, FFN_B_SEMS, P3A_SEMS,
                P3B_SEMS):
        for n in lst:
            if n not in names:
                names.append(n)
    return names


def build_fused(nc=None):
    nc = nc or bass.Bass("TRN2", target_bir_lowering=False)
    xT = nc.dram_tensor("xT", [1024, 8192], F32, kind="ExternalInput")
    WkvT = nc.dram_tensor("WkvT", [1024, 2048], F32, kind="ExternalInput")
    bkv = nc.dram_tensor("bkv", [1, 2048], F32, kind="ExternalInput")
    WqT = nc.dram_tensor("WqT", [1024, 1024], F32, kind="ExternalInput")
    bq = nc.dram_tensor("bq", [1, 1024], F32, kind="ExternalInput")
    WoT = nc.dram_tensor("WoT", [1024, 1024], F32, kind="ExternalInput")
    bo = nc.dram_tensor("bo", [1, 1024], F32, kind="ExternalInput")
    W1T_0 = nc.dram_tensor("W1T_0", [1024, 4096], F32, kind="ExternalInput")
    b1_0 = nc.dram_tensor("b1_0", [1, 4096], F32, kind="ExternalInput")
    W2T_0 = nc.dram_tensor("W2T_0", [4096, 1024], F32, kind="ExternalInput")
    b2_0 = nc.dram_tensor("b2_0", [1, 1024], F32, kind="ExternalInput")
    w0_d = nc.dram_tensor("w0", [1024, 4096], F32, kind="ExternalInput")
    w1_d = nc.dram_tensor("w1", [4096, 1024], F32, kind="ExternalInput")
    w2_d = nc.dram_tensor("w2", [1024, 4096], F32, kind="ExternalInput")
    W1T_1 = nc.dram_tensor("W1T_1", [1024, 4096], F32, kind="ExternalInput")
    b1_1 = nc.dram_tensor("b1_1", [1, 4096], F32, kind="ExternalInput")
    W2T_1 = nc.dram_tensor("W2T_1", [4096, 1024], F32, kind="ExternalInput")
    b2_1 = nc.dram_tensor("b2_1", [1, 1024], F32, kind="ExternalInput")
    outT = nc.dram_tensor("outT", [1024, 8192], F32, kind="ExternalOutput")
    E_d = nc.inline_tensor(make_E(), "Emat")
    ctx_i = nc.dram_tensor("ctx_i", [128, 1040], BF16, kind="Internal")
    r1_i = nc.dram_tensor("r1_i", [1024, 8192], F32, kind="Internal")
    r2_i = nc.dram_tensor("r2_i", [1024, 8192], F32, kind="Internal")
    r3_i = nc.dram_tensor("r3_i", [1024, 8192], F32, kind="Internal")
    h1_i = nc.dram_tensor("h1_i", [4096, 8192], BF16, kind="Internal")
    h2_i = nc.dram_tensor("h2_i", [4096, 8192], BF16, kind="Internal")
    u_i = nc.dram_tensor("u_i", [4096, 8192], BF16, kind="Internal")
    t_i = nc.dram_tensor("t_i", [4096, 8192], BF16, kind="Internal")
    scr = [nc.dram_tensor(f"scr{i}", [1, 128], F32, kind="Internal")
           for i in range(4)]
    cc_in = [nc.dram_tensor(f"cci{c}", [1, 8], F32, kind="Internal")
             for c in range(8)]
    cc_out = [nc.dram_tensor(f"cco{c}", [1, 8], F32, kind="Internal")
              for c in range(8)]
    with ExitStack() as es:
        sm = mk_sems(nc, es, fused_sem_names(), "f")
        emit_p0(nc, es, xT, WkvT, bkv, ctx_i)
        nc.all_engine_barrier()
        emit_p1(nc, es, sm, xT, ctx_i, WqT, bq, WoT, bo, E_d, r1_i)
        nc.all_engine_barrier()
        emit_ffn_a(nc, es, sm, r1_i, W1T_0, b1_0, scr[0], h1_i,
                       AF.Gelu, "ternary", pfx="fa1")
        nc.all_engine_barrier()
        emit_ffn_b(nc, es, sm, r1_i, h1_i, W2T_0, b2_0, scr[1], r2_i,
                       "ternary", pfx="fb1")
        nc.all_engine_barrier()
        emit_p3a(nc, es, sm, r2_i, w0_d, w2_d, u_i, t_i)
        nc.all_engine_barrier()
        emit_p3b(nc, es, sm, r2_i, u_i, t_i, w1_d, cc_in, cc_out, r3_i)
        nc.all_engine_barrier()
        emit_ffn_a(nc, es, sm, r3_i, W1T_1, b1_1, scr[2], h2_i,
                       AF.Relu, "quant", pfx="fa2")
        nc.all_engine_barrier()
        emit_ffn_b(nc, es, sm, r3_i, h2_i, W2T_1, b2_1, scr[3], outT,
                       "quant", pfx="fb2")
    return nc




def kernel(**inputs):
    ii = {k: np.asarray(v, dtype=np.float32) for k, v in inputs.items()}
    x = ii["L0_attn_x"]
    cores = list(range(N_CORES))
    nc = build_fused()
    in_maps = []
    shared = {
        "WkvT": prep_wkv(ii["L0_attn_Wk"], ii["L0_attn_Wv"]),
        "bkv": prep_bkv(ii["L0_attn_bk"], ii["L0_attn_bv"]),
        "WqT": prep_wT(ii["L0_attn_Wq"]), "bq": prep_row(ii["L0_attn_bq"]),
        "WoT": prep_wT(ii["L0_attn_Wo"]), "bo": prep_row(ii["L0_attn_bo"]),
        "W1T_0": prep_wT(ii["L0_ffn_W1"]), "b1_0": prep_row(ii["L0_ffn_b1"]),
        "W2T_0": prep_wT(ii["L0_ffn_W2"]), "b2_0": prep_row(ii["L0_ffn_b2"]),
        "w0": np.ascontiguousarray(ii["L1_attn_w0"]),
        "w1": np.ascontiguousarray(ii["L1_attn_w1"]),
        "w2": np.ascontiguousarray(ii["L1_attn_w2"]),
        "W1T_1": prep_wT(ii["L1_ffn_W1"]), "b1_1": prep_row(ii["L1_ffn_b1"]),
        "W2T_1": prep_wT(ii["L1_ffn_W2"]), "b2_1": prep_row(ii["L1_ffn_b2"]),
    }
    for c in cores:
        m = dict(shared)
        m["xT"] = prep_core_x(x, c)
        in_maps.append(m)
    res = run_bass_kernel_spmd(nc, in_maps, core_ids=cores)
    out = np.empty((16, 4096, 1024), np.float32)
    for c in cores:
        out[2 * c:2 * c + 2] = np.asarray(
            res.results[c]["outT"]).T.reshape(2, 4096, 1024)
    return out
